# version 13 (final)
# baseline (speedup 1.0000x reference)
import numpy as np
import concourse.bass as bass
import concourse.mybir as mybir
from concourse.bass_utils import run_bass_kernel_spmd

F32 = mybir.dt.float32
F32R = mybir.dt.float32r
I32 = mybir.dt.int32
AF = mybir.ActivationFunctionType
ALU = mybir.AluOpType
AX = mybir.AxisListType

D = 1024
NST = 16
NC = 8
TT = 128
TR = 256
EPS = 1e-5
VC = 4097.0


def _set_len(l):
    global L, TSL, NTT, NT2, NTR
    L = l
    TSL = L // NC
    NTT = L // TT
    NT2 = TSL // TT
    NTR = L // TR


_set_len(16384)


def _bcast_ap(sl, nblk, rep):
    pitch = 1
    for s in list(sl.tensor.shape)[1:]:
        pitch *= int(s)
    return bass.AP(sl.tensor, sl.offset, [[pitch, 128], [1, nblk], [0, rep]])


def emit_ln_stats(vector, st, xt, dfree):
    half = dfree // 2
    vector.bn_stats(st[:, 0:6], xt[:, 0:half])
    vector.bn_stats(st[:, 6:12], xt[:, half:dfree])
    vector.drain()
    vector.bn_aggr(st[:, 12:14], st[:, 0:12])
    vector.drain()
    r = vector.tensor_single_scalar(st[:, 14:15], st[:, 13:14], EPS, ALU.add)
    vector.drain()
    return r


def emit_ln_finish(vector, st):
    vector.reciprocal(st[:, 14:15], st[:, 15:16])
    vector.drain()
    vector.tensor_tensor(st[:, 15:16], st[:, 12:13], st[:, 14:15], ALU.mult)
    vector.drain()
    r = vector.tensor_single_scalar(st[:, 15:16], st[:, 15:16], -1.0, ALU.mult)
    vector.drain()
    return r


def emit_quant(vector, x_in, xd_out, scr, iscr, amax_t, scale_t, nblk):
    W = 32 * nblk
    v = scr[:, 0:W]
    g = scr[:, W:2 * W]
    r = scr[:, 2 * W:3 * W]
    s = scr[:, 3 * W:4 * W]
    amax = amax_t[:, 0:nblk]
    scale = scale_t[:, 0:nblk]
    x3 = x_in.rearrange("p (a b) -> p a b", b=32)
    steps = [
        lambda: vector.tensor_reduce(amax.rearrange("p (a o) -> p a o", o=1), x3,
                                     AX.X, ALU.max, apply_absolute_value=True),
        lambda: vector.tensor_scalar(scale, amax, 1.0 / 7.5, 1e-30, ALU.mult, ALU.max),
        lambda: vector.reciprocal(amax, scale),
        lambda: vector.tensor_tensor(v.rearrange("p (a b) -> p a b", b=32), x3,
                                     _bcast_ap(amax, nblk, 32), ALU.mult),
        lambda: vector.tensor_tensor(g, v, v, ALU.mult),
        lambda: vector.tensor_scalar(r, g, 16.0, -2.0, ALU.is_ge, ALU.mult),
        lambda: vector.tensor_single_scalar(g, g, 4.0, ALU.is_ge),
        lambda: vector.tensor_scalar(s, r, -0.125, 0.125, ALU.mult, ALU.add),
        lambda: vector.scalar_tensor_tensor(s, g, 0.125, s, ALU.mult, ALU.add),
        lambda: vector.tensor_scalar(g, g, -4.0, 8.0, ALU.mult, ALU.add),
        lambda: vector.tensor_tensor(r, r, g, ALU.add),
        lambda: vector.tensor_tensor(v, v, r, ALU.mult),
        lambda: vector.tensor_copy(iscr[:, 0:W], v),
        lambda: vector.tensor_copy(g, iscr[:, 0:W]),
        lambda: vector.tensor_tensor(r, v, g, ALU.subtract),
        lambda: vector.tensor_single_scalar(v, r, 0.5, ALU.is_ge),
        lambda: vector.tensor_single_scalar(r, r, -0.5, ALU.is_le),
        lambda: vector.tensor_tensor(g, g, v, ALU.add),
        lambda: vector.tensor_tensor(g, g, r, ALU.subtract),
        lambda: vector.tensor_tensor(s.rearrange("p (a b) -> p a b", b=32),
                                     s.rearrange("p (a b) -> p a b", b=32),
                                     _bcast_ap(scale, nblk, 32), ALU.mult),
    ]
    for f in steps:
        f()
        vector.drain()
    out = vector.tensor_tensor(xd_out, g, s, ALU.mult)
    vector.drain()
    return out

def emit_quant_pair(vector, xs, scr, iscr, amax_t, scale_t, nblk):
    W = 32 * nblk

    def steps(h, x_in, xd_out):
        v = scr[:, (0 + h) * W:(1 + h) * W]
        g = scr[:, (2 + h) * W:(3 + h) * W]
        r = scr[:, (4 + h) * W:(5 + h) * W]
        s = scr[:, (6 + h) * W:(7 + h) * W]
        ic = iscr[:, h * W:(h + 1) * W]
        amax = amax_t[:, h * nblk:(h + 1) * nblk]
        scale = scale_t[:, h * nblk:(h + 1) * nblk]
        x3 = x_in.rearrange("p (a b) -> p a b", b=32)
        return [
            lambda: vector.tensor_reduce(amax.rearrange("p (a o) -> p a o", o=1),
                                         x3, AX.X, ALU.max, apply_absolute_value=True),
            lambda: vector.tensor_scalar(scale, amax, 1.0 / 7.5, 1e-30, ALU.mult, ALU.max),
            lambda: vector.reciprocal(amax, scale),
            lambda: vector.tensor_tensor(v.rearrange("p (a b) -> p a b", b=32), x3,
                                         _bcast_ap(amax, nblk, 32), ALU.mult),
            lambda: vector.tensor_tensor(g, v, v, ALU.mult),
            lambda: vector.tensor_scalar(r, g, 16.0, -2.0, ALU.is_ge, ALU.mult),
            lambda: vector.tensor_single_scalar(g, g, 4.0, ALU.is_ge),
            lambda: vector.tensor_scalar(s, r, -0.125, 0.125, ALU.mult, ALU.add),
            lambda: vector.scalar_tensor_tensor(s, g, 0.125, s, ALU.mult, ALU.add),
            lambda: vector.tensor_scalar(g, g, -4.0, 8.0, ALU.mult, ALU.add),
            lambda: vector.tensor_tensor(r, r, g, ALU.add),
            lambda: vector.tensor_tensor(v, v, r, ALU.mult),
            lambda: vector.tensor_copy(ic, v),
            lambda: vector.tensor_copy(g, ic),
            lambda: vector.tensor_tensor(r, v, g, ALU.subtract),
            lambda: vector.tensor_single_scalar(v, r, 0.5, ALU.is_ge),
            lambda: vector.tensor_single_scalar(r, r, -0.5, ALU.is_le),
            lambda: vector.tensor_tensor(g, g, v, ALU.add),
            lambda: vector.tensor_tensor(g, g, r, ALU.subtract),
            lambda: vector.tensor_tensor(s.rearrange("p (a b) -> p a b", b=32),
                                         s.rearrange("p (a b) -> p a b", b=32),
                                         _bcast_ap(scale, nblk, 32), ALU.mult),
            lambda: vector.tensor_tensor(xd_out, g, s, ALU.mult),
        ]

    sa = steps(0, xs[0][0], xs[0][1])
    sb = steps(1, xs[1][0], xs[1][1])
    for fa, fb in zip(sa, sb):
        fa()
        fb()
        vector.drain()




def build_l1():
    nc = bass.Bass("TRN2", target_bir_lowering=False, detect_race_conditions=False)

    x_d = nc.dram_tensor("x", [L, D], F32, kind="ExternalInput")
    xot_d = nc.dram_tensor("xot", [L, 128], F32, kind="ExternalInput")
    wxpT_d = nc.dram_tensor("wxpT", [128, 8 * 96], F32, kind="ExternalInput")
    wdtT_d = nc.dram_tensor("wdtT", [64, 128], F32, kind="ExternalInput")
    bdt_d = nc.dram_tensor("bdt", [128, 1], F32, kind="ExternalInput")
    alog_d = nc.dram_tensor("alog", [128, NST], F32, kind="ExternalInput")
    dvec_d = nc.dram_tensor("dvec", [128, 1], F32, kind="ExternalInput")
    oh_d = nc.dram_tensor("oh32", [32, 32 * 128], F32R, kind="ExternalInput")
    ident_d = nc.dram_tensor("ident", [128, 128], F32, kind="ExternalInput")
    wffn_d = nc.dram_tensor("wffn", [128, D], F32, kind="ExternalInput")

    y_d = nc.dram_tensor("y_own", [128, L], F32, kind="ExternalOutput")
    wq_d = nc.dram_tensor("wq_own", [128, D], F32, kind="ExternalOutput")
    xn_i = nc.dram_tensor("xn_own", [128, L], F32, kind="Internal")
    warm_i = nc.dram_tensor("warm_i", [128, 4 * 256], F32, kind="Internal")

    xdblT = nc.alloc_sbuf_tensor("xdblT", [96, L], F32)
    dt_sb = nc.alloc_sbuf_tensor("dt_sb", [128, L], F32)
    wdtT = nc.alloc_sbuf_tensor("wdtT_s", [64, 128], F32)
    bdt = nc.alloc_sbuf_tensor("bdt_s", [128, 1], F32)
    aneg = nc.alloc_sbuf_tensor("aneg_s", [128, NST], F32)
    dvec = nc.alloc_sbuf_tensor("dvec_s", [128, 1], F32)

    sem_w = nc.alloc_semaphore("sem_w")

    with nc.named_scope("L1_b0_consts"), nc.Block() as b0:
        @b0.sync
        def _(sync):
            for dst, src in [(wdtT, wdtT_d), (bdt, bdt_d), (aneg, alog_d),
                             (dvec, dvec_d)]:
                sync.dma_start(dst[:], src[:]).then_inc(sem_w, 16)

        @b0.scalar
        def _(scalar):
            scalar.wait_ge(sem_w, 4 * 16)
            scalar.activation(aneg[:], aneg[:], AF.Exp).then_inc(sem_w, 1)

        @b0.vector
        def _(vector):
            vector.wait_ge(sem_w, 4 * 16 + 1)
            vector.tensor_single_scalar(aneg[:], aneg[:], -1.0, ALU.mult)

    ld = nc.alloc_semaphore("ld")
    sv1 = nc.alloc_semaphore("sv1")
    ss1 = nc.alloc_semaphore("ss1")
    sv2 = nc.alloc_semaphore("sv2")
    ss2 = nc.alloc_semaphore("ss2")
    st1 = nc.alloc_semaphore("st1")
    sev = nc.alloc_semaphore("sev")
    st2 = nc.alloc_semaphore("st2")
    sevp = nc.alloc_semaphore("sevp")
    sow = nc.alloc_semaphore("sow")

    with (
        nc.sbuf_tensor("wxpT_s", [128, 8 * 96], F32) as wxpT,
        nc.sbuf_tensor("ident_s", [128, 128], F32) as ident,
        nc.sbuf_tensor("x_tile", [128, 4 * D], F32) as x_tile,
        nc.sbuf_tensor("xo_tile", [128, 4 * 128], F32) as xo_tile,
        nc.sbuf_tensor("xn_tile", [128, 4 * D], F32) as xn_tile,
        nc.sbuf_tensor("xno_tile", [128, 4 * 128], F32) as xno_tile,
        nc.sbuf_tensor("stats", [128, 4 * 16], F32) as stats,
        nc.sbuf_tensor("xnT_buf", [128, 4 * D], F32) as xnT_buf,
        nc.sbuf_tensor("own_strip", [128, 512], F32) as own_strip,
        nc.psum_tensor("ps_tp", [128, 2 * 1536], F32) as ps_tp,
        nc.psum_tensor("ps_pj", [96, 2 * 512], F32) as ps_pj,
        nc.named_scope("L1_p1_ln_proj"), nc.Block() as p1,
    ):
        @p1.sync
        def _(sync):
            sync.dma_start(wxpT[:], wxpT_d[:]).then_inc(sem_w, 16)
            sync.dma_start(ident[:], ident_d[:]).then_inc(sem_w, 16)
            for t in range(NTT):
                b = t % 4
                if t >= 4:
                    sync.wait_ge(ss2, t - 3)
                sync.dma_start(x_tile[:, b * D:(b + 1) * D],
                               x_d[t * TT:(t + 1) * TT, :]).then_inc(ld, 16)
                sync.dma_start(xo_tile[:, b * 128:(b + 1) * 128],
                               xot_d[t * TT:(t + 1) * TT, :]).then_inc(ld, 16)

        @p1.gpsimd
        def _(gpsimd):
            for g in range(NTT // 4):
                gpsimd.wait_ge(sev, 4 * g + 4)
                gpsimd.dma_start(xn_i[:, g * 512:(g + 1) * 512],
                                 own_strip[:]).then_inc(sow, 16)

        @p1.vector
        def _(vector):
            for t in range(NTT + 3):
                do_s = t < NTT
                do_f = 1 <= t <= NTT
                if do_s:
                    b = t % 4
                    st = stats[:, b * 16:(b + 1) * 16]
                    vector.wait_ge(ld, 32 * (t + 1))
                if do_f:
                    i = t - 1
                    bf = i % 4
                    sf = stats[:, bf * 16:(bf + 1) * 16]
                    vector.wait_ge(ss1, i + 1)
                if do_s and do_f:
                    xt = x_tile[:, b * D:(b + 1) * D]
                    vector.bn_stats(st[:, 0:6], xt[:, 0:D // 2])
                    vector.bn_stats(st[:, 6:12], xt[:, D // 2:D])
                    vector.reciprocal(sf[:, 14:15], sf[:, 15:16])
                    vector.drain()
                    vector.bn_aggr(st[:, 12:14], st[:, 0:12])
                    vector.tensor_tensor(sf[:, 15:16], sf[:, 12:13], sf[:, 14:15],
                                         ALU.mult)
                    vector.drain()
                    vector.tensor_single_scalar(st[:, 14:15], st[:, 13:14], EPS,
                                                ALU.add).then_inc(sv1, 1)
                    vector.tensor_single_scalar(sf[:, 15:16], sf[:, 15:16], -1.0,
                                                ALU.mult).then_inc(sv2, 1)
                    vector.drain()
                elif do_s:
                    emit_ln_stats(vector, st, x_tile[:, b * D:(b + 1) * D], D
                                  ).then_inc(sv1, 1)
                elif do_f:
                    emit_ln_finish(vector, sf).then_inc(sv2, 1)
                if 3 <= t <= NTT + 2:
                    i = t - 3
                    vector.wait_ge(st2, i + 1)
                    vector.tensor_copy(xdblT[:, i * TT:(i + 1) * TT],
                                       ps_pj[:, (i % 2) * 512:(i % 2) * 512 + 128]
                                       ).then_inc(sevp, 1)

        @p1.scalar
        def _(scalar):
            for t in range(NTT + 2):
                if t < NTT:
                    b = t % 4
                    st = stats[:, b * 16:(b + 1) * 16]
                    scalar.wait_ge(sv1, t + 1)
                    scalar.activation(st[:, 15:16], st[:, 14:15], AF.Sqrt).then_inc(ss1, 1)
                if 1 <= t <= NTT:
                    i = t - 1
                    b = i % 4
                    st = stats[:, b * 16:(b + 1) * 16]
                    scalar.wait_ge(sv2, i + 1)
                    if i >= 4:
                        scalar.wait_ge(st1, i - 3)
                    scalar.activation(xn_tile[:, b * D:(b + 1) * D],
                                      x_tile[:, b * D:(b + 1) * D], AF.Identity,
                                      scale=st[:, 14:15], bias=st[:, 15:16])
                    scalar.activation(xno_tile[:, b * 128:(b + 1) * 128],
                                      xo_tile[:, b * 128:(b + 1) * 128], AF.Identity,
                                      scale=st[:, 14:15], bias=st[:, 15:16]).then_inc(ss2, 1)
                if t >= 2:
                    i = t - 2
                    b = i % 4
                    scalar.wait_ge(st1, i + 1)
                    if i % 4 == 0 and i >= 4:
                        scalar.wait_ge(sow, (i // 4) * 16)
                    scalar.copy(xnT_buf[:, b * D:(b + 1) * D],
                                ps_tp[:, (i % 2) * 1536:(i % 2) * 1536 + 1024])
                    scalar.copy(own_strip[:, (i % 4) * 128:(i % 4 + 1) * 128],
                                ps_tp[:, (i % 2) * 1536 + 1024:(i % 2) * 1536 + 1152]
                                ).then_inc(sev, 1)

        @p1.tensor
        def _(tensor):
            tensor.wait_ge(sem_w, 4 * 16 + 1 + 32)
            for t in range(NTT + 2):
                if 1 <= t <= NTT:
                    i = t - 1
                    b = i % 4
                    rg = i % 2
                    tensor.wait_ge(ss2, i + 1)
                    if i >= 2:
                        tensor.wait_ge(sev, i - 1)
                    for c in range(8):
                        tensor.transpose(ps_tp[:, rg * 1536 + c * 128:rg * 1536 + (c + 1) * 128],
                                         xn_tile[:, b * D + c * 128:b * D + (c + 1) * 128],
                                         ident[:])
                    tensor.transpose(ps_tp[:, rg * 1536 + 1024:rg * 1536 + 1152],
                                     xno_tile[:, b * 128:(b + 1) * 128],
                                     ident[:]).then_inc(st1, 1)
                if t >= 2:
                    i = t - 2
                    b = i % 4
                    rg = i % 2
                    tensor.wait_ge(sev, i + 1)
                    if i >= 2:
                        tensor.wait_ge(sevp, i - 1)
                    for c in range(8):
                        mm = tensor.matmul(ps_pj[:, rg * 512:rg * 512 + 128],
                                           wxpT[:, c * 96:(c + 1) * 96],
                                           xnT_buf[:, b * D + c * 128:b * D + (c + 1) * 128],
                                           start=(c == 0), stop=(c == 7))
                        if c == 7:
                            mm.then_inc(st2, 1)

    sdm = nc.alloc_semaphore("sdm")
    sds = nc.alloc_semaphore("sds")
    sde = nc.alloc_semaphore("sde")
    with (
        nc.sbuf_tensor("sp1", [128, 2 * 512], F32) as sp1,
        nc.sbuf_tensor("sp2", [128, 2 * 512], F32) as sp2,
        nc.psum_tensor("ps_dt", [128, 2 * 512], F32) as ps_dt,
        nc.named_scope("L1_b2_dt"), nc.Block() as b2,
    ):
        @b2.tensor
        def _(tensor):
            for r in range(L // 512):
                b = r % 2
                if r >= 2:
                    tensor.wait_ge(sds, r - 1)
                tensor.matmul(ps_dt[:, b * 512:(b + 1) * 512], wdtT[:],
                              xdblT[0:64, r * 512:(r + 1) * 512],
                              start=True, stop=True).then_inc(sdm, 1)

        @b2.scalar
        def _(scalar):
            for r in range(L // 512):
                b = r % 2
                ps = ps_dt[:, b * 512:(b + 1) * 512]
                c1 = sp1[:, b * 512:(b + 1) * 512]
                c2 = sp2[:, b * 512:(b + 1) * 512]
                scalar.wait_ge(sdm, r + 1)
                scalar.activation(c1, ps, AF.Abs, bias=bdt[:, 0:1])
                scalar.drain()
                scalar.activation(c2, c1, AF.Exp, scale=-1.0)
                scalar.drain()
                scalar.activation(c2, c2, AF.Ln, bias=1.0)
                scalar.drain()
                scalar.activation(c1, ps, AF.Relu, bias=bdt[:, 0:1]).then_inc(sds, 1)

        @b2.vector
        def _(vector):
            for r in range(L // 512):
                b = r % 2
                vector.wait_ge(sds, r + 1)
                vector.tensor_add(dt_sb[:, r * 512:(r + 1) * 512],
                                  sp1[:, b * 512:(b + 1) * 512],
                                  sp2[:, b * 512:(b + 1) * 512]).then_inc(sde, 1)

    swq = nc.alloc_semaphore("swq")
    with (
        nc.sbuf_tensor("wffn_s", [128, D], F32) as wffn,
        nc.sbuf_tensor("wscr", [128, 4 * D], F32) as wscr,
        nc.sbuf_tensor("wint", [128, D], I32) as wint,
        nc.sbuf_tensor("wam", [128, D // 32], F32) as wam,
        nc.sbuf_tensor("wsc", [128, D // 32], F32) as wsc,
        nc.sbuf_tensor("wqv", [128, D], F32) as wqv,
        nc.named_scope("L1_b3_wq"), nc.Block() as b3,
    ):
        @b3.sync
        def _(sync):
            sync.dma_start(wffn[:], wffn_d[:]).then_inc(swq, 16)

        @b3.vector
        def _(vector):
            vector.wait_ge(swq, 16)
            emit_quant(vector, wffn[:], wqv[:], wscr, wint, wam, wsc, D // 32)
            nc.vector.engine_nop().then_inc(swq, 1)

        @b3.sync
        def _(sync):
            sync.wait_ge(swq, 17)
            sync.dma_start(wq_d[:], wqv[:]).then_inc(swq, 16)

    sxn = nc.alloc_semaphore("sxn")
    sexp = nc.alloc_semaphore("sexp")
    srep = nc.alloc_semaphore("srep")
    svn = nc.alloc_semaphore("svn")
    svs = nc.alloc_semaphore("svs")
    syv = nc.alloc_semaphore("syv")
    syw = nc.alloc_semaphore("syw")

    with (
        nc.sbuf_tensor("oh96_s", [96, 32 * 128], F32R) as oh96,
        nc.sbuf_tensor("xnr", [128, 2 * TR], F32) as xnr,
        nc.sbuf_tensor("dtx", [128, TR], F32) as dtx,
        nc.sbuf_tensor("bct", [96, TR], F32) as bct96,
        nc.sbuf_tensor("bcs", [96, TR], F32) as bcs96,
        nc.sbuf_tensor("bchi", [96, TR], F32R) as bchi96,
        nc.sbuf_tensor("bclo", [96, TR], F32R) as bclo96,
        nc.sbuf_tensor("abar", [128, 4 * TR], F32) as abar,
        nc.sbuf_tensor("ub", [128, 2 * TR], F32) as ub,
        nc.sbuf_tensor("hbuf", [128, 2 * NST * TR], F32) as hbuf,
        nc.sbuf_tensor("hcb", [128, NST * TR], F32) as hcb,
        nc.sbuf_tensor("ybuf", [128, 2 * TR], F32) as ybuf,
        nc.psum_tensor("ps_bc", [128, 8 * 512], F32) as ps_bc,
        nc.named_scope("L1_b4_scan"), nc.Block() as b4,
    ):
        sabg = nc.alloc_semaphore("sabg")

        @b4.sync
        def _(sync):
            sync.wait_ge(swq, 33)
            sync.dma_start(oh96[64:96, :], oh_d[:]).then_inc(sxn, 16)
            sync.wait_ge(sexp, 4)
            sync.dma_start(warm_i[:], abar[:]).then_inc(sabg, 16)
            for r in range(NTR):
                b = r % 2
                if r >= 2:
                    sync.wait_ge(syv, r - 1)
                sync.dma_start(xnr[:, b * TR:(b + 1) * TR],
                               xn_i[:, r * TR:(r + 1) * TR]).then_inc(sxn, 16)

        @b4.gpsimd
        def _(gpsimd):
            for r in range(NTR):
                b = r % 2
                gpsimd.wait_ge(syv, r + 1)
                gpsimd.dma_start(y_d[:, r * TR:(r + 1) * TR],
                                 ybuf[:, b * TR:(b + 1) * TR]).then_inc(syw, 16)

        @b4.vector
        def _(vector):
            for r in range(NTR):
                b = r % 2
                xr = xnr[:, b * TR:(b + 1) * TR]
                hcur = hbuf[:, b * NST * TR:(b + 1) * NST * TR]
                hprv = hbuf[:, (1 - b) * NST * TR:(2 - b) * NST * TR]
                vector.wait_ge(sxn, (r + 2) * 16)
                if r >= 2:
                    vector.wait_ge(syw, (r - 1) * 16)
                vector.tensor_mul(dtx[:], dt_sb[:, r * TR:(r + 1) * TR], xr)
                bc_in = xdblT[64:96, r * TR:(r + 1) * TR]
                bct = bct96[64:96, :]
                bcs = bcs96[64:96, :]
                vector.tensor_single_scalar(bct, bc_in, VC, ALU.mult)
                vector.tensor_sub(bcs, bct, bc_in)
                vector.tensor_sub(bchi96[64:96, :], bct, bcs)
                vector.tensor_sub(bclo96[64:96, :], bc_in, bchi96[64:96, :])
                nc.vector.engine_nop().then_inc(svs, 1)
                for n in range(NST):
                    p = n % 4
                    pu = n % 2
                    i = r * NST + n
                    vector.wait_ge(sexp, i + 1)
                    vector.wait_ge(srep, 2 * i + 2)
                    u = ub[:, pu * TR:(pu + 1) * TR]
                    vector.tensor_mul(u, dtx[:], ps_bc[:, (2 * p) * 512:(2 * p) * 512 + TR])
                    vector.tensor_tensor_scan(
                        hcur[:, n * TR:(n + 1) * TR],
                        abar[:, p * TR:(p + 1) * TR], u,
                        (0.0 if r == 0 else hprv[:, n * TR + TR - 1:n * TR + TR]),
                        ALU.mult, ALU.add)
                    vector.tensor_mul(hcb[:, n * TR:(n + 1) * TR],
                                      hcur[:, n * TR:(n + 1) * TR],
                                      ps_bc[:, (2 * p + 1) * 512:(2 * p + 1) * 512 + TR]
                                      ).then_inc(svn, 1)
                for lvl in (8, 4, 2, 1):
                    vector.tensor_add(hcb[:, 0:lvl * TR], hcb[:, 0:lvl * TR],
                                      hcb[:, lvl * TR:2 * lvl * TR])
                vector.scalar_tensor_tensor(ybuf[:, b * TR:(b + 1) * TR],
                                            xr, dvec[:, 0:1], hcb[:, 0:TR],
                                            ALU.mult, ALU.add).then_inc(syv, 1)

        @b4.scalar
        def _(scalar):
            for r in range(NTR):
                for n in range(NST):
                    p = n % 4
                    i = r * NST + n
                    if i >= 4:
                        scalar.wait_ge(svn, i - 3)
                    if i == 4:
                        scalar.wait_ge(sabg, 16)
                    scalar.activation(abar[:, p * TR:(p + 1) * TR],
                                      dt_sb[:, r * TR:(r + 1) * TR], AF.Exp,
                                      scale=aneg[:, n:n + 1]).then_inc(sexp, 1)

        @b4.tensor
        def _(tensor):
            tensor.wait_ge(sxn, 16)
            for r in range(NTR):
                tensor.wait_ge(svs, r + 1)
                for n in range(NST):
                    p = n % 4
                    i = r * NST + n
                    if i >= 4:
                        tensor.wait_ge(svn, i - 3)
                    for half, hl in ((0, bchi96), (1, bclo96)):
                        tensor.matmul(ps_bc[:, (2 * p) * 512:(2 * p) * 512 + TR],
                                      oh96[64:96, n * 128:(n + 1) * 128],
                                      hl[64:96, :],
                                      start=(half == 0), stop=(half == 1))
                    m = None
                    for half, hl in ((0, bchi96), (1, bclo96)):
                        m = tensor.matmul(ps_bc[:, (2 * p + 1) * 512:(2 * p + 1) * 512 + TR],
                                          oh96[64:96, (16 + n) * 128:(17 + n) * 128],
                                          hl[64:96, :],
                                          start=(half == 0), stop=(half == 1))
                    m.then_inc(srep, 2)

    with nc.named_scope("L1_b5_fin"), nc.Block() as b5:
        @b5.sync
        def _(sync):
            sync.wait_ge(syw, NTR * 16)
            sync.wait_ge(swq, 33)
            sync.wait_ge(sow, (NTT // 4) * 16)

    return nc



def build_l2():
    nc = bass.Bass("TRN2", target_bir_lowering=False, detect_race_conditions=False)

    xs_d = nc.dram_tensor("xsl", [TSL, D], F32, kind="ExternalInput")
    yf_d = nc.dram_tensor("yfm", [D, TSL], F32, kind="ExternalInput")
    wq_d = nc.dram_tensor("wqcm", [128, 8 * D], F32, kind="ExternalInput")
    bias_d = nc.dram_tensor("biascm", [128, 8], F32, kind="ExternalInput")
    ident_d = nc.dram_tensor("ident", [128, 128], F32, kind="ExternalInput")
    out_d = nc.dram_tensor("out", [TSL, D], F32, kind="ExternalOutput")
    x1_i = nc.dram_tensor("x1_i", [TSL, D], F32, kind="Internal")

    ident = nc.alloc_sbuf_tensor("ident2", [128, 128], F32)
    wqT = nc.alloc_sbuf_tensor("wqT", [128, 8 * D], F32)
    biascm = nc.alloc_sbuf_tensor("bias2", [128, 8], F32)
    xdT = nc.alloc_sbuf_tensor("xdT", [128, 8 * 1024], F32)

    sw = nc.alloc_semaphore("sw")
    swt = nc.alloc_semaphore("swt")
    swe = nc.alloc_semaphore("swe")

    ld = nc.alloc_semaphore("ld2")
    lg = nc.alloc_semaphore("lg2")
    sv1 = nc.alloc_semaphore("sv1b")
    ss1 = nc.alloc_semaphore("ss1b")
    sv2 = nc.alloc_semaphore("sv2b")
    sty = nc.alloc_semaphore("sty")
    sey = nc.alloc_semaphore("sey")
    sq1 = nc.alloc_semaphore("sq1")
    stx = nc.alloc_semaphore("stx")
    sex = nc.alloc_semaphore("sex")
    sx1 = nc.alloc_semaphore("sx1")
    sfm = nc.alloc_semaphore("sfm")
    sfe = nc.alloc_semaphore("sfe")
    stb = nc.alloc_semaphore("stb")
    seb = nc.alloc_semaphore("seb")
    sv3 = nc.alloc_semaphore("sv3")
    ss3 = nc.alloc_semaphore("ss3")
    sv4 = nc.alloc_semaphore("sv4")
    lx1 = nc.alloc_semaphore("lx1")
    sot = nc.alloc_semaphore("sot")

    NG = NT2 // 4

    with (
        nc.sbuf_tensor("yfm_g", [128, 2 * 8 * 512], F32) as yfm_g,
        nc.sbuf_tensor("x_tile2", [128, 2 * D], F32) as x_tile,
        nc.sbuf_tensor("ytm", [128, D], F32) as ytm,
        nc.sbuf_tensor("stats2", [128, 2 * 32], F32) as stats2,
        nc.sbuf_tensor("stats3", [128, 2 * 16], F32) as stats3,
        nc.sbuf_tensor("w1", [128, 2 * D], F32) as w1,
        nc.sbuf_tensor("w2", [128, 2 * D], F32) as w2,
        nc.sbuf_tensor("qscr", [128, 8 * D], F32) as qscr,
        nc.sbuf_tensor("qint", [128, 2 * D], I32) as qint,
        nc.sbuf_tensor("qam", [128, 2 * (D // 32)], F32) as qam,
        nc.sbuf_tensor("qsc", [128, 2 * (D // 32)], F32) as qsc,
        nc.sbuf_tensor("ffnT", [128, 8 * 512], F32) as ffnT,
        nc.sbuf_tensor("ftm", [128, 4 * D], F32) as ftm,
        nc.sbuf_tensor("x1t", [128, 2 * D], F32) as x1t,
        nc.psum_tensor("ps_y", [128, 1024], F32) as ps_y,
        nc.psum_tensor("ps_x", [128, 1024], F32) as ps_x,
        nc.psum_tensor("ps_f", [128, 2 * 512], F32) as ps_f,
        nc.psum_tensor("ps_b", [128, 1024], F32) as ps_b,
        nc.named_scope("L2_b1_main"), nc.Block() as b1,
    ):
        @b1.sync
        def _(sync):
            sync.dma_start(ident[:], ident_d[:]).then_inc(sw, 16)
            sync.dma_start(biascm[:], bias_d[:]).then_inc(sw, 16)
            sync.dma_start(qscr[:, 0:8 * D], wq_d[:]).then_inc(sw, 16)
            for t in range(NT2):
                b = t % 2
                if t % 4 == 0:
                    g = t // 4
                    gb = g % 2
                    if g >= 2:
                        sync.wait_ge(sty, (g - 1) * 4 + 1)
                    for c in range(8):
                        sync.dma_start(
                            yfm_g[:, gb * 4096 + c * 512:gb * 4096 + (c + 1) * 512],
                            yf_d[c * 128:(c + 1) * 128, g * 512:(g + 1) * 512]
                        ).then_inc(lg, 16)
                if t >= 2:
                    sync.wait_ge(sv2, t - 1)
                sync.dma_start(x_tile[:, b * D:(b + 1) * D],
                               xs_d[t * TT:(t + 1) * TT, :]).then_inc(ld, 16)
                if t >= 4:
                    j = t - 4
                    jb = j % 2
                    sync.wait_ge(sx1, (j + 1) * 16)
                    if j >= 2:
                        sync.wait_ge(sot, (j - 1) * 16)
                    sync.dma_start(x1t[:, jb * D:(jb + 1) * D],
                                   x1_i[j * TT:(j + 1) * TT, :]).then_inc(lx1, 16)
            for j in range(NT2 - 4, NT2):
                jb = j % 2
                sync.wait_ge(sx1, (j + 1) * 16)
                if j >= 2:
                    sync.wait_ge(sot, (j - 1) * 16)
                sync.dma_start(x1t[:, jb * D:(jb + 1) * D],
                               x1_i[j * TT:(j + 1) * TT, :]).then_inc(lx1, 16)

        @b1.gpsimd
        def _(gpsimd):
            def outdma(j):
                jb = j % 2
                gpsimd.wait_ge(sv4, j + 1)
                gpsimd.dma_start(out_d[j * TT:(j + 1) * TT, :],
                                 x1t[:, jb * D:(jb + 1) * D]).then_inc(sot, 16)

            for t in range(NT2):
                b = t % 2
                gpsimd.wait_ge(sv2, t + 1)
                gpsimd.dma_start(x1_i[t * TT:(t + 1) * TT, :],
                                 w1[:, b * D:(b + 1) * D]).then_inc(sx1, 16)
                if t % 4 == 1 and t >= 4:
                    for j in range((t // 4 - 1) * 4, (t // 4) * 4):
                        outdma(j)
            for j in range(NT2 - 4, NT2):
                outdma(j)

        @b1.tensor
        def _(tensor):
            tensor.wait_ge(sw, 48)
            for gi in range(16):
                gb = gi % 2
                if gi >= 2:
                    tensor.wait_ge(swe, gi - 1)
                m = None
                for q in range(4):
                    i = gi * 4 + q
                    dc, kc = divmod(i, 8)
                    m = tensor.transpose(
                        ps_f[:, gb * 512 + q * 128:gb * 512 + (q + 1) * 128],
                        qscr[:, dc * D + kc * 128:dc * D + (kc + 1) * 128],
                        ident[:])
                m.then_inc(swt, 1)

            def ytp(t):
                g, j = divmod(t, 4)
                gb = g % 2
                tensor.wait_ge(lg, (g + 1) * 8 * 16)
                if t >= 1:
                    tensor.wait_ge(sey, t)
                m = None
                for c in range(8):
                    m = tensor.transpose(
                        ps_y[:, c * 128:(c + 1) * 128],
                        yfm_g[:, gb * 4096 + c * 512 + j * 128:gb * 4096 + c * 512 + (j + 1) * 128],
                        ident[:])
                m.then_inc(sty, 1)

            ytp(0)
            for t in range(NT2):
                b = t % 2
                if t + 1 < NT2:
                    ytp(t + 1)
                tensor.wait_ge(sq1, t + 1)
                if t >= 1:
                    tensor.wait_ge(sex, t)
                m = None
                for c in range(8):
                    m = tensor.transpose(
                        ps_x[:, c * 128:(c + 1) * 128],
                        w2[:, b * D + c * 128:b * D + (c + 1) * 128],
                        ident[:])
                m.then_inc(stx, 1)
                if t % 4 == 3:
                    g = t // 4
                    gb = g % 2
                    if g == 0:
                        tensor.wait_ge(swe, 16)
                    tensor.wait_ge(sex, t + 1)
                    for mchunk in range(8):
                        i = g * 8 + mchunk
                        p = i % 2
                        if i >= 2:
                            tensor.wait_ge(sfe, i - 1)
                        m = None
                        for kc in range(8):
                            m = tensor.matmul(
                                ps_f[:, p * 512:(p + 1) * 512],
                                wqT[:, kc * D + mchunk * 128:kc * D + (mchunk + 1) * 128],
                                xdT[:, kc * 1024 + (g % 2) * 512:kc * 1024 + (g % 2) * 512 + 512],
                                start=(kc == 0), stop=(kc == 7))
                        m.then_inc(sfm, 1)
                    for j4 in range(4):
                        j = g * 4 + j4
                        jb = j % 2
                        tensor.wait_ge(sfe, g * 8 + 8)
                        if j >= 1:
                            tensor.wait_ge(seb, j)
                        m = None
                        for mc in range(8):
                            m = tensor.transpose(
                                ps_b[:, mc * 128:(mc + 1) * 128],
                                ffnT[:, mc * 512 + j4 * 128:mc * 512 + (j4 + 1) * 128],
                                ident[:])
                        m.then_inc(stb, 1)

        @b1.scalar
        def _(scalar):
            for gi in range(16):
                gb = gi % 2
                scalar.wait_ge(swt, gi + 1)
                last = None
                for q in range(4):
                    i = gi * 4 + q
                    dc, kc = divmod(i, 8)
                    last = scalar.copy(wqT[:, kc * D + dc * 128:kc * D + (dc + 1) * 128],
                                       ps_f[:, gb * 512 + q * 128:gb * 512 + (q + 1) * 128])
                last.then_inc(swe, 1)

            def tile_head(t):
                b = t % 2
                st = stats2[:, b * 32:b * 32 + 16]
                st2 = stats2[:, b * 32 + 16:b * 32 + 32]
                scalar.wait_ge(sty, t + 1)
                if t >= 1:
                    scalar.wait_ge(sv1, 2 * t)
                scalar.copy(ytm[:, 0:D],
                            ps_y[:, 0:1024]).then_inc(sey, 1)
                scalar.wait_ge(sv1, 2 * t + 1)
                scalar.activation(st[:, 15:16], st[:, 14:15], AF.Sqrt).then_inc(ss1, 1)
                scalar.wait_ge(sv1, 2 * t + 2)
                scalar.activation(st2[:, 15:16], st2[:, 14:15], AF.Sqrt).then_inc(ss1, 1)

            def xd_evac(t):
                scalar.wait_ge(stx, t + 1)
                g_, j4_ = divmod(t, 4)
                if g_ >= 2:
                    scalar.wait_ge(sfm, (g_ - 1) * 8)
                scalar.copy(
                    xdT[:].rearrange("p (c q) -> p c q", c=8)[:, :, (g_ % 2) * 512 + j4_ * 128:(g_ % 2) * 512 + (j4_ + 1) * 128],
                    ps_x[:, 0:1024].rearrange("p (c q) -> p c q", q=128)
                ).then_inc(sex, 1)

            for t0 in range(0, NT2, 2):
                t = t0 + 1
                tile_head(t0)
                tile_head(t)
                xd_evac(t0)
                xd_evac(t)
                if t % 4 == 3:
                    g = t // 4
                    gb = g % 2
                    for mchunk in range(8):
                        i = g * 8 + mchunk
                        p = i % 2
                        if g >= 1:
                            scalar.wait_ge(stb, g * 4)
                        scalar.wait_ge(sfm, i + 1)
                        scalar.activation(
                            ffnT[:, mchunk * 512:(mchunk + 1) * 512],
                            ps_f[:, p * 512:(p + 1) * 512], AF.Identity,
                            bias=biascm[:, mchunk:mchunk + 1]).then_inc(sfe, 1)
                    for j4 in range(4):
                        j = g * 4 + j4
                        scalar.wait_ge(stb, j + 1)
                        if j >= 4:
                            scalar.wait_ge(sv4, j - 3)
                        scalar.copy(ftm[:, (j % 4) * D:(j % 4 + 1) * D],
                                    ps_b[:, 0:1024]).then_inc(seb, 1)
                if t % 4 == 1 and t >= 4:
                    for j in range((t // 4 - 1) * 4, (t // 4) * 4):
                        st3 = stats3[:, (j % 2) * 16:(j % 2 + 1) * 16]
                        scalar.wait_ge(sv3, j + 1)
                        scalar.activation(st3[:, 15:16], st3[:, 14:15],
                                          AF.Sqrt).then_inc(ss3, 1)
            for j in range(NT2 - 4, NT2):
                st3 = stats3[:, (j % 2) * 16:(j % 2 + 1) * 16]
                scalar.wait_ge(sv3, j + 1)
                scalar.activation(st3[:, 15:16], st3[:, 14:15],
                                  AF.Sqrt).then_inc(ss3, 1)

        @b1.vector
        def _(vector):
            def resid(j):
                jb = j % 2
                st3 = stats3[:, jb * 16:(jb + 1) * 16]
                ft = ftm[:, (j % 4) * D:(j % 4 + 1) * D]
                vector.wait_ge(seb, j + 1)
                emit_ln_stats(vector, st3, ft, D).then_inc(sv3, 1)
                vector.wait_ge(ss3, j + 1)
                emit_ln_finish(vector, st3)
                vector.wait_ge(lx1, (j + 1) * 16)
                nrm = qscr[:, 0:D]
                vector.tensor_scalar(nrm, ft,
                                     st3[:, 12:13], st3[:, 14:15],
                                     ALU.subtract, ALU.mult)
                vector.drain()
                vector.tensor_add(x1t[:, jb * D:(jb + 1) * D],
                                  x1t[:, jb * D:(jb + 1) * D],
                                  nrm).then_inc(sv4, 1)

            def ln_tile(t):
                b = t % 2
                st = stats2[:, b * 32:b * 32 + 16]
                st2 = stats2[:, b * 32 + 16:b * 32 + 32]
                yt = ytm[:, 0:D]
                xt = x_tile[:, b * D:(b + 1) * D]
                w1t = w1[:, b * D:(b + 1) * D]
                w2t = w2[:, b * D:(b + 1) * D]
                vector.wait_ge(sey, t + 1)
                vector.wait_ge(ld, (t + 1) * 16)
                if t >= 2:
                    vector.wait_ge(sx1, (t - 1) * 16)
                    vector.wait_ge(sex, t - 1)
                emit_ln_stats(vector, st, yt, D).then_inc(sv1, 1)
                vector.wait_ge(ss1, 2 * t + 1)
                emit_ln_finish(vector, st)
                vector.tensor_scalar(w1t, yt, st[:, 12:13], st[:, 14:15],
                                     ALU.subtract, ALU.mult)
                vector.drain()
                vector.tensor_add(w1t, w1t, xt)
                vector.drain()
                emit_ln_stats(vector, st2, w1t, D).then_inc(sv1, 1)
                vector.wait_ge(ss1, 2 * t + 2)
                emit_ln_finish(vector, st2)
                vector.tensor_scalar(w2t, w1t, st2[:, 12:13], st2[:, 14:15],
                                     ALU.subtract, ALU.mult).then_inc(sv2, 1)
                vector.drain()
                return w2t

            for t0 in range(0, NT2, 2):
                t1 = t0 + 1
                wa = ln_tile(t0)
                wb = ln_tile(t1)
                if t0 == 0:
                    vector.wait_ge(swe, 16)
                emit_quant_pair(vector, [(wa, wa), (wb, wb)],
                                qscr, qint, qam, qsc, D // 32)
                nc.vector.engine_nop().then_inc(sq1, 1)
                nc.vector.engine_nop().then_inc(sq1, 1)
                if t1 % 4 == 1 and t1 >= 4:
                    for j in range((t1 // 4 - 1) * 4, (t1 // 4) * 4):
                        resid(j)
            for j in range(NT2 - 4, NT2):
                resid(j)

    with nc.named_scope("L2_b3_fin"), nc.Block() as b3:
        @b3.sync
        def _(sync):
            sync.wait_ge(sot, NT2 * 16)

    return nc




def kernel(L0_attn_x, L0_attn_A_log, L0_attn_D, L0_attn_W_xproj, L0_attn_W_dt,
           L0_attn_b_dt, L0_ffn_weight, L0_ffn_bias):
    _set_len(int(np.asarray(L0_attn_x).size) // D)
    f32 = np.float32
    x = np.ascontiguousarray(np.asarray(L0_attn_x, dtype=f32).reshape(L, D))
    alog = np.asarray(L0_attn_A_log, dtype=f32)
    dv = np.asarray(L0_attn_D, dtype=f32)
    wxp = np.asarray(L0_attn_W_xproj, dtype=f32)
    wdt = np.asarray(L0_attn_W_dt, dtype=f32)
    bdt = np.asarray(L0_attn_b_dt, dtype=f32)
    wffn = np.asarray(L0_ffn_weight, dtype=f32)
    bffn = np.asarray(L0_ffn_bias, dtype=f32)

    ident = np.eye(128, dtype=f32)
    wxpT = np.ascontiguousarray(
        wxp.T.reshape(8, 128, 96).transpose(1, 0, 2).reshape(128, 8 * 96))
    oh32 = np.zeros((32, 32 * 128), dtype=f32)
    for j in range(32):
        oh32[j, j * 128:(j + 1) * 128] = 1.0

    nc1 = build_l1()
    in_maps1 = []
    for k in range(NC):
        sl = slice(128 * k, 128 * (k + 1))
        in_maps1.append({
            "x": x,
            "xot": np.ascontiguousarray(x[:, sl]),
            "wxpT": wxpT,
            "wdtT": np.ascontiguousarray(wdt[sl, :].T),
            "bdt": np.ascontiguousarray(bdt[sl].reshape(128, 1)),
            "alog": np.ascontiguousarray(alog[sl, :]),
            "dvec": np.ascontiguousarray(dv[sl].reshape(128, 1)),
            "oh32": oh32,
            "ident": ident,
            "wffn": np.ascontiguousarray(wffn[sl, :]),
        })
    res1 = run_bass_kernel_spmd(nc1, in_maps1, core_ids=list(range(NC)))

    Y = np.concatenate([np.asarray(res1.results[k]["y_own"]) for k in range(NC)], axis=0)
    WQ = np.concatenate([np.asarray(res1.results[k]["wq_own"]) for k in range(NC)], axis=0)
    wqcm = np.ascontiguousarray(
        WQ.reshape(8, 128, D).transpose(1, 0, 2).reshape(128, 8 * D))
    biascm = np.ascontiguousarray(bffn.reshape(8, 128).T)

    nc2 = build_l2()
    in_maps2 = []
    for k in range(NC):
        ts = slice(TSL * k, TSL * (k + 1))
        in_maps2.append({
            "xsl": np.ascontiguousarray(x[ts, :]),
            "yfm": np.ascontiguousarray(Y[:, ts]),
            "wqcm": wqcm,
            "biascm": biascm,
            "ident": ident,
        })
    res2 = run_bass_kernel_spmd(nc2, in_maps2, core_ids=list(range(NC)))

    out = np.concatenate([np.asarray(res2.results[k]["out"]) for k in range(NC)], axis=0)
    return out.reshape(1, L, D).astype(np.float32)


# version 14
# speedup vs baseline: 1.0016x; 1.0016x over previous
import numpy as np
import concourse.bass as bass
import concourse.mybir as mybir
from concourse.bass_utils import run_bass_kernel_spmd

F32 = mybir.dt.float32
F32R = mybir.dt.float32r
I32 = mybir.dt.int32
AF = mybir.ActivationFunctionType
ALU = mybir.AluOpType
AX = mybir.AxisListType

D = 1024
NST = 16
NC = 8
TT = 128
TR = 256
EPS = 1e-5
VC = 4097.0


def _set_len(l):
    global L, TSL, NTT, NT2, NTR
    L = l
    TSL = L // NC
    NTT = L // TT
    NT2 = TSL // TT
    NTR = L // TR


_set_len(16384)


def _bcast_ap(sl, nblk, rep):
    pitch = 1
    for s in list(sl.tensor.shape)[1:]:
        pitch *= int(s)
    return bass.AP(sl.tensor, sl.offset, [[pitch, 128], [1, nblk], [0, rep]])


def emit_ln_stats(vector, st, xt, dfree):
    half = dfree // 2
    vector.bn_stats(st[:, 0:6], xt[:, 0:half])
    vector.bn_stats(st[:, 6:12], xt[:, half:dfree])
    vector.drain()
    vector.bn_aggr(st[:, 12:14], st[:, 0:12])
    vector.drain()
    r = vector.tensor_single_scalar(st[:, 14:15], st[:, 13:14], EPS, ALU.add)
    return r


def emit_ln_finish(vector, st):
    vector.reciprocal(st[:, 14:15], st[:, 15:16])
    vector.drain()
    vector.tensor_tensor(st[:, 15:16], st[:, 12:13], st[:, 14:15], ALU.mult)
    vector.drain()
    r = vector.tensor_single_scalar(st[:, 15:16], st[:, 15:16], -1.0, ALU.mult)
    vector.drain()
    return r


def emit_quant(vector, x_in, xd_out, scr, iscr, amax_t, scale_t, nblk):
    W = 32 * nblk
    v = scr[:, 0:W]
    g = scr[:, W:2 * W]
    r = scr[:, 2 * W:3 * W]
    s = scr[:, 3 * W:4 * W]
    amax = amax_t[:, 0:nblk]
    scale = scale_t[:, 0:nblk]
    x3 = x_in.rearrange("p (a b) -> p a b", b=32)
    steps = [
        lambda: vector.tensor_reduce(amax.rearrange("p (a o) -> p a o", o=1), x3,
                                     AX.X, ALU.max, apply_absolute_value=True),
        lambda: vector.tensor_scalar(scale, amax, 1.0 / 7.5, 1e-30, ALU.mult, ALU.max),
        lambda: vector.reciprocal(amax, scale),
        lambda: vector.tensor_tensor(v.rearrange("p (a b) -> p a b", b=32), x3,
                                     _bcast_ap(amax, nblk, 32), ALU.mult),
        lambda: vector.tensor_tensor(g, v, v, ALU.mult),
        lambda: vector.tensor_scalar(r, g, 16.0, -2.0, ALU.is_ge, ALU.mult),
        lambda: vector.tensor_single_scalar(g, g, 4.0, ALU.is_ge),
        lambda: vector.tensor_scalar(s, r, -0.125, 0.125, ALU.mult, ALU.add),
        lambda: vector.scalar_tensor_tensor(s, g, 0.125, s, ALU.mult, ALU.add),
        lambda: vector.tensor_scalar(g, g, -4.0, 8.0, ALU.mult, ALU.add),
        lambda: vector.tensor_tensor(r, r, g, ALU.add),
        lambda: vector.tensor_tensor(v, v, r, ALU.mult),
        lambda: vector.tensor_copy(iscr[:, 0:W], v),
        lambda: vector.tensor_copy(g, iscr[:, 0:W]),
        lambda: vector.tensor_tensor(r, v, g, ALU.subtract),
        lambda: vector.tensor_single_scalar(v, r, 0.5, ALU.is_ge),
        lambda: vector.tensor_single_scalar(r, r, -0.5, ALU.is_le),
        lambda: vector.tensor_tensor(g, g, v, ALU.add),
        lambda: vector.tensor_tensor(g, g, r, ALU.subtract),
        lambda: vector.tensor_tensor(s.rearrange("p (a b) -> p a b", b=32),
                                     s.rearrange("p (a b) -> p a b", b=32),
                                     _bcast_ap(scale, nblk, 32), ALU.mult),
    ]
    for f in steps:
        f()
        vector.drain()
    out = vector.tensor_tensor(xd_out, g, s, ALU.mult)
    vector.drain()
    return out

def emit_quant_pair(vector, xs, scr, iscr, amax_t, scale_t, nblk):
    W = 32 * nblk

    def steps(h, x_in, xd_out):
        v = scr[:, (0 + h) * W:(1 + h) * W]
        g = scr[:, (2 + h) * W:(3 + h) * W]
        r = scr[:, (4 + h) * W:(5 + h) * W]
        s = scr[:, (6 + h) * W:(7 + h) * W]
        ic = iscr[:, h * W:(h + 1) * W]
        amax = amax_t[:, h * nblk:(h + 1) * nblk]
        scale = scale_t[:, h * nblk:(h + 1) * nblk]
        x3 = x_in.rearrange("p (a b) -> p a b", b=32)
        return [
            lambda: vector.tensor_reduce(amax.rearrange("p (a o) -> p a o", o=1),
                                         x3, AX.X, ALU.max, apply_absolute_value=True),
            lambda: vector.tensor_scalar(scale, amax, 1.0 / 7.5, 1e-30, ALU.mult, ALU.max),
            lambda: vector.reciprocal(amax, scale),
            lambda: vector.tensor_tensor(v.rearrange("p (a b) -> p a b", b=32), x3,
                                         _bcast_ap(amax, nblk, 32), ALU.mult),
            lambda: vector.tensor_tensor(g, v, v, ALU.mult),
            lambda: vector.tensor_scalar(r, g, 16.0, -2.0, ALU.is_ge, ALU.mult),
            lambda: vector.tensor_single_scalar(g, g, 4.0, ALU.is_ge),
            lambda: vector.tensor_scalar(s, r, -0.125, 0.125, ALU.mult, ALU.add),
            lambda: vector.scalar_tensor_tensor(s, g, 0.125, s, ALU.mult, ALU.add),
            lambda: vector.tensor_scalar(g, g, -4.0, 8.0, ALU.mult, ALU.add),
            lambda: vector.tensor_tensor(r, r, g, ALU.add),
            lambda: vector.tensor_tensor(v, v, r, ALU.mult),
            lambda: vector.tensor_copy(ic, v),
            lambda: vector.tensor_copy(g, ic),
            lambda: vector.tensor_tensor(r, v, g, ALU.subtract),
            lambda: vector.tensor_single_scalar(v, r, 0.5, ALU.is_ge),
            lambda: vector.tensor_single_scalar(r, r, -0.5, ALU.is_le),
            lambda: vector.tensor_tensor(g, g, v, ALU.add),
            lambda: vector.tensor_tensor(g, g, r, ALU.subtract),
            lambda: vector.tensor_tensor(s.rearrange("p (a b) -> p a b", b=32),
                                         s.rearrange("p (a b) -> p a b", b=32),
                                         _bcast_ap(scale, nblk, 32), ALU.mult),
            lambda: vector.tensor_tensor(xd_out, g, s, ALU.mult),
        ]

    sa = steps(0, xs[0][0], xs[0][1])
    sb = steps(1, xs[1][0], xs[1][1])
    for fa, fb in zip(sa, sb):
        fa()
        fb()
        vector.drain()




def build_l1():
    nc = bass.Bass("TRN2", target_bir_lowering=False, detect_race_conditions=False)

    x_d = nc.dram_tensor("x", [L, D], F32, kind="ExternalInput")
    xot_d = nc.dram_tensor("xot", [L, 128], F32, kind="ExternalInput")
    wxpT_d = nc.dram_tensor("wxpT", [128, 8 * 96], F32, kind="ExternalInput")
    wdtT_d = nc.dram_tensor("wdtT", [64, 128], F32, kind="ExternalInput")
    bdt_d = nc.dram_tensor("bdt", [128, 1], F32, kind="ExternalInput")
    alog_d = nc.dram_tensor("alog", [128, NST], F32, kind="ExternalInput")
    dvec_d = nc.dram_tensor("dvec", [128, 1], F32, kind="ExternalInput")
    oh_d = nc.dram_tensor("oh32", [32, 32 * 128], F32R, kind="ExternalInput")
    ident_d = nc.dram_tensor("ident", [128, 128], F32, kind="ExternalInput")
    wffn_d = nc.dram_tensor("wffn", [128, D], F32, kind="ExternalInput")

    y_d = nc.dram_tensor("y_own", [128, L], F32, kind="ExternalOutput")
    wq_d = nc.dram_tensor("wq_own", [128, D], F32, kind="ExternalOutput")
    xn_i = nc.dram_tensor("xn_own", [128, L], F32, kind="Internal")
    warm_i = nc.dram_tensor("warm_i", [128, 4 * 256], F32, kind="Internal")

    xdblT = nc.alloc_sbuf_tensor("xdblT", [96, L], F32)
    dt_sb = nc.alloc_sbuf_tensor("dt_sb", [128, L], F32)
    wdtT = nc.alloc_sbuf_tensor("wdtT_s", [64, 128], F32)
    bdt = nc.alloc_sbuf_tensor("bdt_s", [128, 1], F32)
    aneg = nc.alloc_sbuf_tensor("aneg_s", [128, NST], F32)
    dvec = nc.alloc_sbuf_tensor("dvec_s", [128, 1], F32)

    sem_w = nc.alloc_semaphore("sem_w")

    with nc.named_scope("L1_b0_consts"), nc.Block() as b0:
        @b0.sync
        def _(sync):
            for dst, src in [(wdtT, wdtT_d), (bdt, bdt_d), (aneg, alog_d),
                             (dvec, dvec_d)]:
                sync.dma_start(dst[:], src[:]).then_inc(sem_w, 16)

        @b0.scalar
        def _(scalar):
            scalar.wait_ge(sem_w, 4 * 16)
            scalar.activation(aneg[:], aneg[:], AF.Exp).then_inc(sem_w, 1)

        @b0.vector
        def _(vector):
            vector.wait_ge(sem_w, 4 * 16 + 1)
            vector.tensor_single_scalar(aneg[:], aneg[:], -1.0, ALU.mult)

    ld = nc.alloc_semaphore("ld")
    sv1 = nc.alloc_semaphore("sv1")
    ss1 = nc.alloc_semaphore("ss1")
    sv2 = nc.alloc_semaphore("sv2")
    ss2 = nc.alloc_semaphore("ss2")
    st1 = nc.alloc_semaphore("st1")
    sev = nc.alloc_semaphore("sev")
    st2 = nc.alloc_semaphore("st2")
    sevp = nc.alloc_semaphore("sevp")
    sow = nc.alloc_semaphore("sow")

    with (
        nc.sbuf_tensor("wxpT_s", [128, 8 * 96], F32) as wxpT,
        nc.sbuf_tensor("ident_s", [128, 128], F32) as ident,
        nc.sbuf_tensor("x_tile", [128, 4 * D], F32) as x_tile,
        nc.sbuf_tensor("xo_tile", [128, 4 * 128], F32) as xo_tile,
        nc.sbuf_tensor("xn_tile", [128, 4 * D], F32) as xn_tile,
        nc.sbuf_tensor("xno_tile", [128, 4 * 128], F32) as xno_tile,
        nc.sbuf_tensor("stats", [128, 4 * 16], F32) as stats,
        nc.sbuf_tensor("xnT_buf", [128, 4 * D], F32) as xnT_buf,
        nc.sbuf_tensor("own_strip", [128, 512], F32) as own_strip,
        nc.psum_tensor("ps_tp", [128, 2 * 1536], F32) as ps_tp,
        nc.psum_tensor("ps_pj", [96, 2 * 512], F32) as ps_pj,
        nc.named_scope("L1_p1_ln_proj"), nc.Block() as p1,
    ):
        @p1.sync
        def _(sync):
            sync.dma_start(wxpT[:], wxpT_d[:]).then_inc(sem_w, 16)
            sync.dma_start(ident[:], ident_d[:]).then_inc(sem_w, 16)
            for t in range(NTT):
                b = t % 4
                if t >= 4:
                    sync.wait_ge(ss2, t - 3)
                sync.dma_start(x_tile[:, b * D:(b + 1) * D],
                               x_d[t * TT:(t + 1) * TT, :]).then_inc(ld, 16)
                sync.dma_start(xo_tile[:, b * 128:(b + 1) * 128],
                               xot_d[t * TT:(t + 1) * TT, :]).then_inc(ld, 16)

        @p1.gpsimd
        def _(gpsimd):
            for g in range(NTT // 4):
                gpsimd.wait_ge(sev, 4 * g + 4)
                gpsimd.dma_start(xn_i[:, g * 512:(g + 1) * 512],
                                 own_strip[:]).then_inc(sow, 16)

        @p1.vector
        def _(vector):
            for t in range(NTT + 3):
                do_s = t < NTT
                do_f = 1 <= t <= NTT
                if do_s:
                    b = t % 4
                    st = stats[:, b * 16:(b + 1) * 16]
                    vector.wait_ge(ld, 32 * (t + 1))
                if do_f:
                    i = t - 1
                    bf = i % 4
                    sf = stats[:, bf * 16:(bf + 1) * 16]
                    vector.wait_ge(ss1, i + 1)
                if do_s and do_f:
                    xt = x_tile[:, b * D:(b + 1) * D]
                    vector.bn_stats(st[:, 0:6], xt[:, 0:D // 2])
                    vector.bn_stats(st[:, 6:12], xt[:, D // 2:D])
                    vector.reciprocal(sf[:, 14:15], sf[:, 15:16])
                    vector.drain()
                    vector.bn_aggr(st[:, 12:14], st[:, 0:12])
                    vector.tensor_tensor(sf[:, 15:16], sf[:, 12:13], sf[:, 14:15],
                                         ALU.mult)
                    vector.drain()
                    vector.tensor_single_scalar(st[:, 14:15], st[:, 13:14], EPS,
                                                ALU.add).then_inc(sv1, 1)
                    vector.tensor_single_scalar(sf[:, 15:16], sf[:, 15:16], -1.0,
                                                ALU.mult).then_inc(sv2, 1)
                elif do_s:
                    emit_ln_stats(vector, st, x_tile[:, b * D:(b + 1) * D], D
                                  ).then_inc(sv1, 1)
                elif do_f:
                    emit_ln_finish(vector, sf).then_inc(sv2, 1)
                if 3 <= t <= NTT + 2:
                    i = t - 3
                    vector.wait_ge(st2, i + 1)
                    vector.tensor_copy(xdblT[:, i * TT:(i + 1) * TT],
                                       ps_pj[:, (i % 2) * 512:(i % 2) * 512 + 128]
                                       ).then_inc(sevp, 1)

        @p1.scalar
        def _(scalar):
            for t in range(NTT + 2):
                if t < NTT:
                    b = t % 4
                    st = stats[:, b * 16:(b + 1) * 16]
                    scalar.wait_ge(sv1, t + 1)
                    scalar.activation(st[:, 15:16], st[:, 14:15], AF.Sqrt).then_inc(ss1, 1)
                if 1 <= t <= NTT:
                    i = t - 1
                    b = i % 4
                    st = stats[:, b * 16:(b + 1) * 16]
                    scalar.wait_ge(sv2, i + 1)
                    if i >= 4:
                        scalar.wait_ge(st1, i - 3)
                    scalar.activation(xn_tile[:, b * D:(b + 1) * D],
                                      x_tile[:, b * D:(b + 1) * D], AF.Identity,
                                      scale=st[:, 14:15], bias=st[:, 15:16])
                    scalar.activation(xno_tile[:, b * 128:(b + 1) * 128],
                                      xo_tile[:, b * 128:(b + 1) * 128], AF.Identity,
                                      scale=st[:, 14:15], bias=st[:, 15:16]).then_inc(ss2, 1)
                if t >= 2:
                    i = t - 2
                    b = i % 4
                    scalar.wait_ge(st1, i + 1)
                    if i % 4 == 0 and i >= 4:
                        scalar.wait_ge(sow, (i // 4) * 16)
                    scalar.copy(xnT_buf[:, b * D:(b + 1) * D],
                                ps_tp[:, (i % 2) * 1536:(i % 2) * 1536 + 1024])
                    scalar.copy(own_strip[:, (i % 4) * 128:(i % 4 + 1) * 128],
                                ps_tp[:, (i % 2) * 1536 + 1024:(i % 2) * 1536 + 1152]
                                ).then_inc(sev, 1)

        @p1.tensor
        def _(tensor):
            tensor.wait_ge(sem_w, 4 * 16 + 1 + 32)
            for t in range(NTT + 2):
                if 1 <= t <= NTT:
                    i = t - 1
                    b = i % 4
                    rg = i % 2
                    tensor.wait_ge(ss2, i + 1)
                    if i >= 2:
                        tensor.wait_ge(sev, i - 1)
                    for c in range(8):
                        tensor.transpose(ps_tp[:, rg * 1536 + c * 128:rg * 1536 + (c + 1) * 128],
                                         xn_tile[:, b * D + c * 128:b * D + (c + 1) * 128],
                                         ident[:])
                    tensor.transpose(ps_tp[:, rg * 1536 + 1024:rg * 1536 + 1152],
                                     xno_tile[:, b * 128:(b + 1) * 128],
                                     ident[:]).then_inc(st1, 1)
                if t >= 2:
                    i = t - 2
                    b = i % 4
                    rg = i % 2
                    tensor.wait_ge(sev, i + 1)
                    if i >= 2:
                        tensor.wait_ge(sevp, i - 1)
                    for c in range(8):
                        mm = tensor.matmul(ps_pj[:, rg * 512:rg * 512 + 128],
                                           wxpT[:, c * 96:(c + 1) * 96],
                                           xnT_buf[:, b * D + c * 128:b * D + (c + 1) * 128],
                                           start=(c == 0), stop=(c == 7))
                        if c == 7:
                            mm.then_inc(st2, 1)

    sdm = nc.alloc_semaphore("sdm")
    sds = nc.alloc_semaphore("sds")
    sde = nc.alloc_semaphore("sde")
    with (
        nc.sbuf_tensor("sp1", [128, 2 * 512], F32) as sp1,
        nc.sbuf_tensor("sp2", [128, 2 * 512], F32) as sp2,
        nc.psum_tensor("ps_dt", [128, 2 * 512], F32) as ps_dt,
        nc.named_scope("L1_b2_dt"), nc.Block() as b2,
    ):
        @b2.tensor
        def _(tensor):
            for r in range(L // 512):
                b = r % 2
                if r >= 2:
                    tensor.wait_ge(sds, r - 1)
                tensor.matmul(ps_dt[:, b * 512:(b + 1) * 512], wdtT[:],
                              xdblT[0:64, r * 512:(r + 1) * 512],
                              start=True, stop=True).then_inc(sdm, 1)

        @b2.scalar
        def _(scalar):
            for r in range(L // 512):
                b = r % 2
                ps = ps_dt[:, b * 512:(b + 1) * 512]
                c1 = sp1[:, b * 512:(b + 1) * 512]
                c2 = sp2[:, b * 512:(b + 1) * 512]
                scalar.wait_ge(sdm, r + 1)
                scalar.activation(c1, ps, AF.Abs, bias=bdt[:, 0:1])
                scalar.drain()
                scalar.activation(c2, c1, AF.Exp, scale=-1.0)
                scalar.drain()
                scalar.activation(c2, c2, AF.Ln, bias=1.0)
                scalar.drain()
                scalar.activation(c1, ps, AF.Relu, bias=bdt[:, 0:1]).then_inc(sds, 1)

        @b2.vector
        def _(vector):
            for r in range(L // 512):
                b = r % 2
                vector.wait_ge(sds, r + 1)
                vector.tensor_add(dt_sb[:, r * 512:(r + 1) * 512],
                                  sp1[:, b * 512:(b + 1) * 512],
                                  sp2[:, b * 512:(b + 1) * 512]).then_inc(sde, 1)

    swq = nc.alloc_semaphore("swq")
    with (
        nc.sbuf_tensor("wffn_s", [128, D], F32) as wffn,
        nc.sbuf_tensor("wscr", [128, 4 * D], F32) as wscr,
        nc.sbuf_tensor("wint", [128, D], I32) as wint,
        nc.sbuf_tensor("wam", [128, D // 32], F32) as wam,
        nc.sbuf_tensor("wsc", [128, D // 32], F32) as wsc,
        nc.sbuf_tensor("wqv", [128, D], F32) as wqv,
        nc.named_scope("L1_b3_wq"), nc.Block() as b3,
    ):
        @b3.sync
        def _(sync):
            sync.dma_start(wffn[:], wffn_d[:]).then_inc(swq, 16)

        @b3.vector
        def _(vector):
            vector.wait_ge(swq, 16)
            emit_quant(vector, wffn[:], wqv[:], wscr, wint, wam, wsc, D // 32)
            nc.vector.engine_nop().then_inc(swq, 1)

        @b3.sync
        def _(sync):
            sync.wait_ge(swq, 17)
            sync.dma_start(wq_d[:], wqv[:]).then_inc(swq, 16)

    sxn = nc.alloc_semaphore("sxn")
    sexp = nc.alloc_semaphore("sexp")
    srep = nc.alloc_semaphore("srep")
    svn = nc.alloc_semaphore("svn")
    svs = nc.alloc_semaphore("svs")
    syv = nc.alloc_semaphore("syv")
    syw = nc.alloc_semaphore("syw")

    with (
        nc.sbuf_tensor("oh96_s", [96, 32 * 128], F32R) as oh96,
        nc.sbuf_tensor("xnr", [128, 2 * TR], F32) as xnr,
        nc.sbuf_tensor("dtx", [128, TR], F32) as dtx,
        nc.sbuf_tensor("bct", [96, TR], F32) as bct96,
        nc.sbuf_tensor("bcs", [96, TR], F32) as bcs96,
        nc.sbuf_tensor("bchi", [96, TR], F32R) as bchi96,
        nc.sbuf_tensor("bclo", [96, TR], F32R) as bclo96,
        nc.sbuf_tensor("abar", [128, 4 * TR], F32) as abar,
        nc.sbuf_tensor("ub", [128, 2 * TR], F32) as ub,
        nc.sbuf_tensor("hbuf", [128, 2 * NST * TR], F32) as hbuf,
        nc.sbuf_tensor("hcb", [128, NST * TR], F32) as hcb,
        nc.sbuf_tensor("ybuf", [128, 2 * TR], F32) as ybuf,
        nc.psum_tensor("ps_bc", [128, 8 * 512], F32) as ps_bc,
        nc.named_scope("L1_b4_scan"), nc.Block() as b4,
    ):
        sabg = nc.alloc_semaphore("sabg")

        @b4.sync
        def _(sync):
            sync.wait_ge(swq, 33)
            sync.dma_start(oh96[64:96, :], oh_d[:]).then_inc(sxn, 16)
            sync.wait_ge(sexp, 4)
            sync.dma_start(warm_i[:], abar[:]).then_inc(sabg, 16)
            for r in range(NTR):
                b = r % 2
                if r >= 2:
                    sync.wait_ge(syv, r - 1)
                sync.dma_start(xnr[:, b * TR:(b + 1) * TR],
                               xn_i[:, r * TR:(r + 1) * TR]).then_inc(sxn, 16)

        @b4.gpsimd
        def _(gpsimd):
            for r in range(NTR):
                b = r % 2
                gpsimd.wait_ge(syv, r + 1)
                gpsimd.dma_start(y_d[:, r * TR:(r + 1) * TR],
                                 ybuf[:, b * TR:(b + 1) * TR]).then_inc(syw, 16)

        @b4.vector
        def _(vector):
            for r in range(NTR):
                b = r % 2
                xr = xnr[:, b * TR:(b + 1) * TR]
                hcur = hbuf[:, b * NST * TR:(b + 1) * NST * TR]
                hprv = hbuf[:, (1 - b) * NST * TR:(2 - b) * NST * TR]
                vector.wait_ge(sxn, (r + 2) * 16)
                if r >= 2:
                    vector.wait_ge(syw, (r - 1) * 16)
                vector.tensor_mul(dtx[:], dt_sb[:, r * TR:(r + 1) * TR], xr)
                bc_in = xdblT[64:96, r * TR:(r + 1) * TR]
                bct = bct96[64:96, :]
                bcs = bcs96[64:96, :]
                vector.tensor_single_scalar(bct, bc_in, VC, ALU.mult)
                vector.tensor_sub(bcs, bct, bc_in)
                vector.tensor_sub(bchi96[64:96, :], bct, bcs)
                vector.tensor_sub(bclo96[64:96, :], bc_in, bchi96[64:96, :])
                nc.vector.engine_nop().then_inc(svs, 1)
                for n in range(NST):
                    p = n % 4
                    pu = n % 2
                    i = r * NST + n
                    vector.wait_ge(sexp, i + 1)
                    vector.wait_ge(srep, 2 * i + 2)
                    u = ub[:, pu * TR:(pu + 1) * TR]
                    vector.tensor_mul(u, dtx[:], ps_bc[:, (2 * p) * 512:(2 * p) * 512 + TR])
                    vector.tensor_tensor_scan(
                        hcur[:, n * TR:(n + 1) * TR],
                        abar[:, p * TR:(p + 1) * TR], u,
                        (0.0 if r == 0 else hprv[:, n * TR + TR - 1:n * TR + TR]),
                        ALU.mult, ALU.add)
                    vector.tensor_mul(hcb[:, n * TR:(n + 1) * TR],
                                      hcur[:, n * TR:(n + 1) * TR],
                                      ps_bc[:, (2 * p + 1) * 512:(2 * p + 1) * 512 + TR]
                                      ).then_inc(svn, 1)
                for lvl in (8, 4, 2, 1):
                    vector.tensor_add(hcb[:, 0:lvl * TR], hcb[:, 0:lvl * TR],
                                      hcb[:, lvl * TR:2 * lvl * TR])
                vector.scalar_tensor_tensor(ybuf[:, b * TR:(b + 1) * TR],
                                            xr, dvec[:, 0:1], hcb[:, 0:TR],
                                            ALU.mult, ALU.add).then_inc(syv, 1)

        @b4.scalar
        def _(scalar):
            for r in range(NTR):
                for n in range(NST):
                    p = n % 4
                    i = r * NST + n
                    if i >= 4:
                        scalar.wait_ge(svn, i - 3)
                    if i == 4:
                        scalar.wait_ge(sabg, 16)
                    scalar.activation(abar[:, p * TR:(p + 1) * TR],
                                      dt_sb[:, r * TR:(r + 1) * TR], AF.Exp,
                                      scale=aneg[:, n:n + 1]).then_inc(sexp, 1)

        @b4.tensor
        def _(tensor):
            tensor.wait_ge(sxn, 16)
            for r in range(NTR):
                tensor.wait_ge(svs, r + 1)
                for n in range(NST):
                    p = n % 4
                    i = r * NST + n
                    if i >= 4:
                        tensor.wait_ge(svn, i - 3)
                    for half, hl in ((0, bchi96), (1, bclo96)):
                        tensor.matmul(ps_bc[:, (2 * p) * 512:(2 * p) * 512 + TR],
                                      oh96[64:96, n * 128:(n + 1) * 128],
                                      hl[64:96, :],
                                      start=(half == 0), stop=(half == 1))
                    m = None
                    for half, hl in ((0, bchi96), (1, bclo96)):
                        m = tensor.matmul(ps_bc[:, (2 * p + 1) * 512:(2 * p + 1) * 512 + TR],
                                          oh96[64:96, (16 + n) * 128:(17 + n) * 128],
                                          hl[64:96, :],
                                          start=(half == 0), stop=(half == 1))
                    m.then_inc(srep, 2)

    with nc.named_scope("L1_b5_fin"), nc.Block() as b5:
        @b5.sync
        def _(sync):
            sync.wait_ge(syw, NTR * 16)
            sync.wait_ge(swq, 33)
            sync.wait_ge(sow, (NTT // 4) * 16)

    return nc



def build_l2():
    nc = bass.Bass("TRN2", target_bir_lowering=False, detect_race_conditions=False)

    xs_d = nc.dram_tensor("xsl", [TSL, D], F32, kind="ExternalInput")
    yf_d = nc.dram_tensor("yfm", [D, TSL], F32, kind="ExternalInput")
    wq_d = nc.dram_tensor("wqcm", [128, 8 * D], F32, kind="ExternalInput")
    bias_d = nc.dram_tensor("biascm", [128, 8], F32, kind="ExternalInput")
    ident_d = nc.dram_tensor("ident", [128, 128], F32, kind="ExternalInput")
    out_d = nc.dram_tensor("out", [TSL, D], F32, kind="ExternalOutput")
    x1_i = nc.dram_tensor("x1_i", [TSL, D], F32, kind="Internal")

    ident = nc.alloc_sbuf_tensor("ident2", [128, 128], F32)
    wqT = nc.alloc_sbuf_tensor("wqT", [128, 8 * D], F32)
    biascm = nc.alloc_sbuf_tensor("bias2", [128, 8], F32)
    xdT = nc.alloc_sbuf_tensor("xdT", [128, 8 * 1024], F32)

    sw = nc.alloc_semaphore("sw")
    swt = nc.alloc_semaphore("swt")
    swe = nc.alloc_semaphore("swe")

    ld = nc.alloc_semaphore("ld2")
    lg = nc.alloc_semaphore("lg2")
    sv1 = nc.alloc_semaphore("sv1b")
    ss1 = nc.alloc_semaphore("ss1b")
    sv2 = nc.alloc_semaphore("sv2b")
    sty = nc.alloc_semaphore("sty")
    sey = nc.alloc_semaphore("sey")
    sq1 = nc.alloc_semaphore("sq1")
    stx = nc.alloc_semaphore("stx")
    sex = nc.alloc_semaphore("sex")
    sx1 = nc.alloc_semaphore("sx1")
    sfm = nc.alloc_semaphore("sfm")
    sfe = nc.alloc_semaphore("sfe")
    stb = nc.alloc_semaphore("stb")
    seb = nc.alloc_semaphore("seb")
    sv3 = nc.alloc_semaphore("sv3")
    ss3 = nc.alloc_semaphore("ss3")
    sv4 = nc.alloc_semaphore("sv4")
    lx1 = nc.alloc_semaphore("lx1")
    sot = nc.alloc_semaphore("sot")

    NG = NT2 // 4

    with (
        nc.sbuf_tensor("yfm_g", [128, 2 * 8 * 512], F32) as yfm_g,
        nc.sbuf_tensor("x_tile2", [128, 2 * D], F32) as x_tile,
        nc.sbuf_tensor("ytm", [128, D], F32) as ytm,
        nc.sbuf_tensor("stats2", [128, 2 * 32], F32) as stats2,
        nc.sbuf_tensor("stats3", [128, 2 * 16], F32) as stats3,
        nc.sbuf_tensor("w1", [128, 2 * D], F32) as w1,
        nc.sbuf_tensor("w2", [128, 2 * D], F32) as w2,
        nc.sbuf_tensor("qscr", [128, 8 * D], F32) as qscr,
        nc.sbuf_tensor("qint", [128, 2 * D], I32) as qint,
        nc.sbuf_tensor("qam", [128, 2 * (D // 32)], F32) as qam,
        nc.sbuf_tensor("qsc", [128, 2 * (D // 32)], F32) as qsc,
        nc.sbuf_tensor("ffnT", [128, 8 * 512], F32) as ffnT,
        nc.sbuf_tensor("ftm", [128, 4 * D], F32) as ftm,
        nc.sbuf_tensor("x1t", [128, 2 * D], F32) as x1t,
        nc.psum_tensor("ps_y", [128, 1024], F32) as ps_y,
        nc.psum_tensor("ps_x", [128, 1024], F32) as ps_x,
        nc.psum_tensor("ps_f", [128, 2 * 512], F32) as ps_f,
        nc.psum_tensor("ps_b", [128, 1024], F32) as ps_b,
        nc.named_scope("L2_b1_main"), nc.Block() as b1,
    ):
        @b1.sync
        def _(sync):
            sync.dma_start(ident[:], ident_d[:]).then_inc(sw, 16)
            sync.dma_start(biascm[:], bias_d[:]).then_inc(sw, 16)
            sync.dma_start(qscr[:, 0:8 * D], wq_d[:]).then_inc(sw, 16)
            for t in range(NT2):
                b = t % 2
                if t % 4 == 0:
                    g = t // 4
                    gb = g % 2
                    if g >= 2:
                        sync.wait_ge(sty, (g - 1) * 4 + 1)
                    for c in range(8):
                        sync.dma_start(
                            yfm_g[:, gb * 4096 + c * 512:gb * 4096 + (c + 1) * 512],
                            yf_d[c * 128:(c + 1) * 128, g * 512:(g + 1) * 512]
                        ).then_inc(lg, 16)
                if t >= 2:
                    sync.wait_ge(sv2, t - 1)
                sync.dma_start(x_tile[:, b * D:(b + 1) * D],
                               xs_d[t * TT:(t + 1) * TT, :]).then_inc(ld, 16)
                if t >= 4:
                    j = t - 4
                    jb = j % 2
                    sync.wait_ge(sx1, (j + 1) * 16)
                    if j >= 2:
                        sync.wait_ge(sot, (j - 1) * 16)
                    sync.dma_start(x1t[:, jb * D:(jb + 1) * D],
                                   x1_i[j * TT:(j + 1) * TT, :]).then_inc(lx1, 16)
            for j in range(NT2 - 4, NT2):
                jb = j % 2
                sync.wait_ge(sx1, (j + 1) * 16)
                if j >= 2:
                    sync.wait_ge(sot, (j - 1) * 16)
                sync.dma_start(x1t[:, jb * D:(jb + 1) * D],
                               x1_i[j * TT:(j + 1) * TT, :]).then_inc(lx1, 16)

        @b1.gpsimd
        def _(gpsimd):
            def outdma(j):
                jb = j % 2
                gpsimd.wait_ge(sv4, j + 1)
                gpsimd.dma_start(out_d[j * TT:(j + 1) * TT, :],
                                 x1t[:, jb * D:(jb + 1) * D]).then_inc(sot, 16)

            for t in range(NT2):
                b = t % 2
                gpsimd.wait_ge(sv2, t + 1)
                gpsimd.dma_start(x1_i[t * TT:(t + 1) * TT, :],
                                 w1[:, b * D:(b + 1) * D]).then_inc(sx1, 16)
                if t % 4 == 1 and t >= 4:
                    for j in range((t // 4 - 1) * 4, (t // 4) * 4):
                        outdma(j)
            for j in range(NT2 - 4, NT2):
                outdma(j)

        @b1.tensor
        def _(tensor):
            tensor.wait_ge(sw, 48)
            for gi in range(16):
                gb = gi % 2
                if gi >= 2:
                    tensor.wait_ge(swe, gi - 1)
                m = None
                for q in range(4):
                    i = gi * 4 + q
                    dc, kc = divmod(i, 8)
                    m = tensor.transpose(
                        ps_f[:, gb * 512 + q * 128:gb * 512 + (q + 1) * 128],
                        qscr[:, dc * D + kc * 128:dc * D + (kc + 1) * 128],
                        ident[:])
                m.then_inc(swt, 1)

            def ytp(t):
                g, j = divmod(t, 4)
                gb = g % 2
                tensor.wait_ge(lg, (g + 1) * 8 * 16)
                if t >= 1:
                    tensor.wait_ge(sey, t)
                m = None
                for c in range(8):
                    m = tensor.transpose(
                        ps_y[:, c * 128:(c + 1) * 128],
                        yfm_g[:, gb * 4096 + c * 512 + j * 128:gb * 4096 + c * 512 + (j + 1) * 128],
                        ident[:])
                m.then_inc(sty, 1)

            ytp(0)
            for t in range(NT2):
                b = t % 2
                if t + 1 < NT2:
                    ytp(t + 1)
                tensor.wait_ge(sq1, t + 1)
                if t >= 1:
                    tensor.wait_ge(sex, t)
                m = None
                for c in range(8):
                    m = tensor.transpose(
                        ps_x[:, c * 128:(c + 1) * 128],
                        w2[:, b * D + c * 128:b * D + (c + 1) * 128],
                        ident[:])
                m.then_inc(stx, 1)
                if t % 4 == 3:
                    g = t // 4
                    gb = g % 2
                    if g == 0:
                        tensor.wait_ge(swe, 16)
                    tensor.wait_ge(sex, t + 1)
                    for mchunk in range(8):
                        i = g * 8 + mchunk
                        p = i % 2
                        if i >= 2:
                            tensor.wait_ge(sfe, i - 1)
                        m = None
                        for kc in range(8):
                            m = tensor.matmul(
                                ps_f[:, p * 512:(p + 1) * 512],
                                wqT[:, kc * D + mchunk * 128:kc * D + (mchunk + 1) * 128],
                                xdT[:, kc * 1024 + (g % 2) * 512:kc * 1024 + (g % 2) * 512 + 512],
                                start=(kc == 0), stop=(kc == 7))
                        m.then_inc(sfm, 1)
                    for j4 in range(4):
                        j = g * 4 + j4
                        jb = j % 2
                        tensor.wait_ge(sfe, g * 8 + 8)
                        if j >= 1:
                            tensor.wait_ge(seb, j)
                        m = None
                        for mc in range(8):
                            m = tensor.transpose(
                                ps_b[:, mc * 128:(mc + 1) * 128],
                                ffnT[:, mc * 512 + j4 * 128:mc * 512 + (j4 + 1) * 128],
                                ident[:])
                        m.then_inc(stb, 1)

        @b1.scalar
        def _(scalar):
            for gi in range(16):
                gb = gi % 2
                scalar.wait_ge(swt, gi + 1)
                last = None
                for q in range(4):
                    i = gi * 4 + q
                    dc, kc = divmod(i, 8)
                    last = scalar.copy(wqT[:, kc * D + dc * 128:kc * D + (dc + 1) * 128],
                                       ps_f[:, gb * 512 + q * 128:gb * 512 + (q + 1) * 128])
                last.then_inc(swe, 1)

            def tile_head(t):
                b = t % 2
                st = stats2[:, b * 32:b * 32 + 16]
                st2 = stats2[:, b * 32 + 16:b * 32 + 32]
                scalar.wait_ge(sty, t + 1)
                if t >= 1:
                    scalar.wait_ge(sv1, 2 * t)
                scalar.copy(ytm[:, 0:D],
                            ps_y[:, 0:1024]).then_inc(sey, 1)
                scalar.wait_ge(sv1, 2 * t + 1)
                scalar.activation(st[:, 15:16], st[:, 14:15], AF.Sqrt).then_inc(ss1, 1)
                scalar.wait_ge(sv1, 2 * t + 2)
                scalar.activation(st2[:, 15:16], st2[:, 14:15], AF.Sqrt).then_inc(ss1, 1)

            def xd_evac(t):
                scalar.wait_ge(stx, t + 1)
                g_, j4_ = divmod(t, 4)
                if g_ >= 2:
                    scalar.wait_ge(sfm, (g_ - 1) * 8)
                scalar.copy(
                    xdT[:].rearrange("p (c q) -> p c q", c=8)[:, :, (g_ % 2) * 512 + j4_ * 128:(g_ % 2) * 512 + (j4_ + 1) * 128],
                    ps_x[:, 0:1024].rearrange("p (c q) -> p c q", q=128)
                ).then_inc(sex, 1)

            for t0 in range(0, NT2, 2):
                t = t0 + 1
                tile_head(t0)
                tile_head(t)
                xd_evac(t0)
                xd_evac(t)
                if t % 4 == 3:
                    g = t // 4
                    gb = g % 2
                    for mchunk in range(8):
                        i = g * 8 + mchunk
                        p = i % 2
                        if g >= 1:
                            scalar.wait_ge(stb, g * 4)
                        scalar.wait_ge(sfm, i + 1)
                        scalar.activation(
                            ffnT[:, mchunk * 512:(mchunk + 1) * 512],
                            ps_f[:, p * 512:(p + 1) * 512], AF.Identity,
                            bias=biascm[:, mchunk:mchunk + 1]).then_inc(sfe, 1)
                    for j4 in range(4):
                        j = g * 4 + j4
                        scalar.wait_ge(stb, j + 1)
                        if j >= 4:
                            scalar.wait_ge(sv4, j - 3)
                        scalar.copy(ftm[:, (j % 4) * D:(j % 4 + 1) * D],
                                    ps_b[:, 0:1024]).then_inc(seb, 1)
                if t % 4 == 1 and t >= 4:
                    for j in range((t // 4 - 1) * 4, (t // 4) * 4):
                        st3 = stats3[:, (j % 2) * 16:(j % 2 + 1) * 16]
                        scalar.wait_ge(sv3, j + 1)
                        scalar.activation(st3[:, 15:16], st3[:, 14:15],
                                          AF.Sqrt).then_inc(ss3, 1)
            for j in range(NT2 - 4, NT2):
                st3 = stats3[:, (j % 2) * 16:(j % 2 + 1) * 16]
                scalar.wait_ge(sv3, j + 1)
                scalar.activation(st3[:, 15:16], st3[:, 14:15],
                                  AF.Sqrt).then_inc(ss3, 1)

        @b1.vector
        def _(vector):
            def resid(j):
                jb = j % 2
                st3 = stats3[:, jb * 16:(jb + 1) * 16]
                ft = ftm[:, (j % 4) * D:(j % 4 + 1) * D]
                vector.wait_ge(seb, j + 1)
                emit_ln_stats(vector, st3, ft, D).then_inc(sv3, 1)
                vector.wait_ge(ss3, j + 1)
                emit_ln_finish(vector, st3)
                vector.wait_ge(lx1, (j + 1) * 16)
                nrm = qscr[:, 0:D]
                vector.tensor_scalar(nrm, ft,
                                     st3[:, 12:13], st3[:, 14:15],
                                     ALU.subtract, ALU.mult)
                vector.drain()
                vector.tensor_add(x1t[:, jb * D:(jb + 1) * D],
                                  x1t[:, jb * D:(jb + 1) * D],
                                  nrm).then_inc(sv4, 1)

            def ln_tile(t):
                b = t % 2
                st = stats2[:, b * 32:b * 32 + 16]
                st2 = stats2[:, b * 32 + 16:b * 32 + 32]
                yt = ytm[:, 0:D]
                xt = x_tile[:, b * D:(b + 1) * D]
                w1t = w1[:, b * D:(b + 1) * D]
                w2t = w2[:, b * D:(b + 1) * D]
                vector.wait_ge(sey, t + 1)
                vector.wait_ge(ld, (t + 1) * 16)
                if t >= 2:
                    vector.wait_ge(sx1, (t - 1) * 16)
                    vector.wait_ge(sex, t - 1)
                emit_ln_stats(vector, st, yt, D).then_inc(sv1, 1)
                vector.wait_ge(ss1, 2 * t + 1)
                emit_ln_finish(vector, st)
                vector.tensor_scalar(w1t, yt, st[:, 12:13], st[:, 14:15],
                                     ALU.subtract, ALU.mult)
                vector.drain()
                vector.tensor_add(w1t, w1t, xt)
                vector.drain()
                emit_ln_stats(vector, st2, w1t, D).then_inc(sv1, 1)
                vector.wait_ge(ss1, 2 * t + 2)
                emit_ln_finish(vector, st2)
                vector.tensor_scalar(w2t, w1t, st2[:, 12:13], st2[:, 14:15],
                                     ALU.subtract, ALU.mult).then_inc(sv2, 1)
                vector.drain()
                return w2t

            for t0 in range(0, NT2, 2):
                t1 = t0 + 1
                wa = ln_tile(t0)
                wb = ln_tile(t1)
                if t0 == 0:
                    vector.wait_ge(swe, 16)
                emit_quant_pair(vector, [(wa, wa), (wb, wb)],
                                qscr, qint, qam, qsc, D // 32)
                nc.vector.engine_nop().then_inc(sq1, 1)
                nc.vector.engine_nop().then_inc(sq1, 1)
                if t1 % 4 == 1 and t1 >= 4:
                    for j in range((t1 // 4 - 1) * 4, (t1 // 4) * 4):
                        resid(j)
            for j in range(NT2 - 4, NT2):
                resid(j)

    with nc.named_scope("L2_b3_fin"), nc.Block() as b3:
        @b3.sync
        def _(sync):
            sync.wait_ge(sot, NT2 * 16)

    return nc




def kernel(L0_attn_x, L0_attn_A_log, L0_attn_D, L0_attn_W_xproj, L0_attn_W_dt,
           L0_attn_b_dt, L0_ffn_weight, L0_ffn_bias):
    _set_len(int(np.asarray(L0_attn_x).size) // D)
    f32 = np.float32
    x = np.ascontiguousarray(np.asarray(L0_attn_x, dtype=f32).reshape(L, D))
    alog = np.asarray(L0_attn_A_log, dtype=f32)
    dv = np.asarray(L0_attn_D, dtype=f32)
    wxp = np.asarray(L0_attn_W_xproj, dtype=f32)
    wdt = np.asarray(L0_attn_W_dt, dtype=f32)
    bdt = np.asarray(L0_attn_b_dt, dtype=f32)
    wffn = np.asarray(L0_ffn_weight, dtype=f32)
    bffn = np.asarray(L0_ffn_bias, dtype=f32)

    ident = np.eye(128, dtype=f32)
    wxpT = np.ascontiguousarray(
        wxp.T.reshape(8, 128, 96).transpose(1, 0, 2).reshape(128, 8 * 96))
    oh32 = np.zeros((32, 32 * 128), dtype=f32)
    for j in range(32):
        oh32[j, j * 128:(j + 1) * 128] = 1.0

    nc1 = build_l1()
    in_maps1 = []
    for k in range(NC):
        sl = slice(128 * k, 128 * (k + 1))
        in_maps1.append({
            "x": x,
            "xot": np.ascontiguousarray(x[:, sl]),
            "wxpT": wxpT,
            "wdtT": np.ascontiguousarray(wdt[sl, :].T),
            "bdt": np.ascontiguousarray(bdt[sl].reshape(128, 1)),
            "alog": np.ascontiguousarray(alog[sl, :]),
            "dvec": np.ascontiguousarray(dv[sl].reshape(128, 1)),
            "oh32": oh32,
            "ident": ident,
            "wffn": np.ascontiguousarray(wffn[sl, :]),
        })
    res1 = run_bass_kernel_spmd(nc1, in_maps1, core_ids=list(range(NC)))

    Y = np.concatenate([np.asarray(res1.results[k]["y_own"]) for k in range(NC)], axis=0)
    WQ = np.concatenate([np.asarray(res1.results[k]["wq_own"]) for k in range(NC)], axis=0)
    wqcm = np.ascontiguousarray(
        WQ.reshape(8, 128, D).transpose(1, 0, 2).reshape(128, 8 * D))
    biascm = np.ascontiguousarray(bffn.reshape(8, 128).T)

    nc2 = build_l2()
    in_maps2 = []
    for k in range(NC):
        ts = slice(TSL * k, TSL * (k + 1))
        in_maps2.append({
            "xsl": np.ascontiguousarray(x[ts, :]),
            "yfm": np.ascontiguousarray(Y[:, ts]),
            "wqcm": wqcm,
            "biascm": biascm,
            "ident": ident,
        })
    res2 = run_bass_kernel_spmd(nc2, in_maps2, core_ids=list(range(NC)))

    out = np.concatenate([np.asarray(res2.results[k]["out"]) for k in range(NC)], axis=0)
    return out.reshape(1, L, D).astype(np.float32)
